# version 01 (final)
# baseline (speedup 1.0000x reference)
import math
import numpy as np
from contextlib import ExitStack

import concourse.bass as bass
import concourse.mybir as mybir
from concourse.bass_utils import run_bass_kernel_spmd

F32 = mybir.dt.float32
BF16 = mybir.dt.bfloat16
AF = mybir.ActivationFunctionType
ALU = mybir.AluOpType
AX = mybir.AxisListType

NTOK = 2048
NT = NTOK // 128
D = 1024


class Bld:
    def __init__(self, nc, ctx):
        self.nc = nc
        self.ctx = ctx
        self.cnt = {}

    def sem(self, name):
        h = self.ctx.enter_context(self.nc.semaphore(name))
        self.cnt[h] = 0
        return h

    def sb(self, name, shape, dt=F32):
        return self.ctx.enter_context(self.nc.sbuf_tensor(name, shape, dt))

    def ps(self, name, shape, dt=F32):
        return self.ctx.enter_context(self.nc.psum_tensor(name, shape, dt))

    def inc(self, inst, sem, n=1):
        inst.then_inc(sem, n)
        self.cnt[sem] += n
        return self.cnt[sem]

    def wait(self, eng, sem, val=None):
        eng.wait_ge(sem, self.cnt[sem] if val is None else val)


def setup_common(b):
    nc = b.nc
    C = {}
    C["eye_bf"] = b.sb("c_eye_bf", [128, 128], BF16)
    C["eye_f"] = b.sb("c_eye_f", [128, 128], F32)
    C["triu_f"] = b.sb("c_triu_f", [128, 128], F32)
    C["tril_f"] = b.sb("c_tril_f", [128, 128], F32)
    C["ones_r_bf"] = b.sb("c_ones_r_bf", [1, 512], BF16)
    C["ones_c_bf"] = b.sb("c_ones_c_bf", [128, 1], BF16)
    C["ones_r_f"] = b.sb("c_ones_r_f", [1, 128], F32)
    C["consts"] = b.sb("c_consts", [128, 8], F32)
    C["zero_bf"] = b.sb("c_zero_bf", [128, 128], BF16)
    C["ones_f"] = b.sb("c_ones_f", [128, 128], F32)
    s_init = b.sem("s_init")
    ge = nc.gpsimd
    ge.memset(C["zero_bf"][:], 0.0)
    b.inc(ge.memset(C["ones_f"][:], 1.0), s_init, 1)
    b.wait(ge, s_init)
    ge.affine_select(out=C["eye_bf"][:], in_=C["zero_bf"][:], compare_op=ALU.not_equal,
                     fill=1.0, base=0, pattern=[[-1, 128]], channel_multiplier=1)
    ge.affine_select(out=C["triu_f"][:], in_=C["ones_f"][:], compare_op=ALU.is_ge,
                     fill=0.0, base=0, pattern=[[1, 128]], channel_multiplier=-1)
    b.inc(ge.affine_select(out=C["tril_f"][:], in_=C["ones_f"][:], compare_op=ALU.is_ge,
                           fill=0.0, base=0, pattern=[[-1, 128]], channel_multiplier=1),
          s_init, 1)
    b.wait(ge, s_init)
    ge.tensor_copy(C["eye_f"][:], C["eye_bf"][:])
    ge.memset(C["ones_r_bf"][:], 1.0)
    ge.memset(C["ones_c_bf"][:], 1.0)
    ge.memset(C["ones_r_f"][:], 1.0)
    vals = [1.0, 2.0, 0.5, -1.0, 1e-6, 1e-5, 0.0, 3.0]
    for j, v in enumerate(vals):
        inst = ge.memset(C["consts"][:, j:j + 1], v)
        try:
            nc.const_aps.register(F32, v, C["consts"][:, j:j + 1])
        except Exception:
            pass
    b.inc(inst, s_init, 1)
    C["s_init"] = s_init
    return C


def ln_tile2(b, C, out_ap, in_ap, scr, sv, width=D, out2_ap=None, eps=1e-5):
    nc = b.nc
    ve, se = nc.vector, nc.scalar
    s_a, s_b = C["s_ln_a"], C["s_ln_b"]
    sums = sv[:, 0:1]
    negm = sv[:, 1:2]
    q = sv[:, 2:3]
    varp = sv[:, 3:4]
    std = sv[:, 4:5]
    rstd = sv[:, 5:6]
    vr = b.inc(ve.reduce_sum(sums, in_ap, axis=AX.X), s_a, 1)
    ve.wait_ge(s_a, vr)
    va = b.inc(ve.tensor_scalar(negm, sums, -1.0 / width, None, op0=ALU.mult), s_a, 1)
    se.wait_ge(s_a, va)
    vb = b.inc(se.activation(scr[:, 0:width], in_ap, AF.Square, bias=negm, accum_out=q),
               s_b, 1)
    ve.wait_ge(s_b, vb)
    vc = b.inc(ve.tensor_scalar(varp, q, 1.0 / width, eps, op0=ALU.mult, op1=ALU.add),
               s_a, 1)
    se.wait_ge(s_a, vc)
    vd = b.inc(se.activation(std, varp, AF.Sqrt), s_b, 1)
    ve.wait_ge(s_b, vd)
    vr2 = b.inc(ve.reciprocal(rstd, std), s_a, 1)
    ve.wait_ge(s_a, vr2)
    inst = ve.tensor_scalar(out_ap, in_ap, negm, rstd, op0=ALU.add, op1=ALU.mult)
    if out2_ap is not None:
        inst = ve.tensor_scalar(out2_ap, in_ap, negm, rstd, op0=ALU.add, op1=ALU.mult)
    return inst


def load_weight_bf16(b, dst_sb, w_ap, kchunks, ncols, sem):
    src = w_ap.rearrange("(kc p) n -> p kc n", p=128)
    dst = dst_sb[:].rearrange("p (kc n) -> p kc n", kc=kchunks)
    return b.inc(b.nc.gpsimd.dma_start(out=dst, in_=src), sem, 16)


def emit_phase_a(b, C, d, nt=NT, stages=3):
    nc = b.nc
    te, ve, se, ge, sy = nc.tensor, nc.vector, nc.scalar, nc.gpsimd, nc.sync

    s_wl = b.sem("a_wl")
    s_xl = [b.sem("a_xl0"), b.sem("a_xl1")]
    s_tp = b.sem("a_tp")
    s_tv = b.sem("a_tv")
    s_mm = b.sem("a_mm")
    s_ev = b.sem("a_ev")
    s_ac = b.sem("a_ac")
    s_st = b.sem("a_st")
    s_fr = b.sem("a_fr")

    Wk_sb = b.sb("a_Wk", [128, 8 * 1024], BF16)
    Wv_sb = b.sb("a_Wv", [128, 8 * 1024], BF16)
    Wq_sb = b.sb("a_Wq", [128, 8 * 1024], BF16)
    Wo_sb = b.sb("a_Wo", [128, 8 * 1024], BF16)
    bias_sb = b.sb("a_bias", [1, 4 * 1024], BF16)
    for w, sbt in ((d["Wk"], Wk_sb), (d["Wv"], Wv_sb), (d["Wq"], Wq_sb), (d["Wo"], Wo_sb)):
        load_weight_bf16(b, sbt, w, 8, 1024, s_wl)
    for j, bn in enumerate(("bq", "bk", "bv", "bo")):
        b.inc(ge.dma_start(out=bias_sb[:, j * 1024:(j + 1) * 1024],
                           in_=d[bn].rearrange("(o n) -> o n", o=1)), s_wl, 16)
    bq_r = bias_sb[:, 0:1024]
    bk_r = bias_sb[:, 1024:2048]
    bv_r = bias_sb[:, 2048:3072]
    bo_r = bias_sb[:, 3072:4096]

    xt = b.sb("a_xt", [128, 2 * D])
    scr = b.sb("a_scr", [128, D])
    sv = b.sb("a_sv", [128, 8])
    xnb = b.sb("a_xnb", [128, D], BF16)
    xT_own = b.sb("a_xT_own", [128, NT * D], BF16)
    xT_cur = b.sb("a_xT_cur", [128, D], BF16)
    Kf = b.sb("a_Kf", [128, D])
    Vf = b.sb("a_Vf", [128, D])
    mn = b.sb("a_mn", [128, D])
    Kb = b.sb("a_Kb", [128, D], BF16)
    KVb = b.sb("a_KVb", [128, D], BF16)
    kv_row = b.sb("a_kvrow", [1, 2 * D])
    kv_rowb = b.sb("a_kvrowb", [1, 2 * D], BF16)
    kv_bc = b.sb("a_kvbc", [128, 2 * D])
    Qf = b.sb("a_Qf", [128, D])
    t1 = b.sb("a_t1", [128, D])
    gsc = b.sb("a_gsc", [128, D])
    at_out = b.sb("a_atout", [128, D])

    psT0 = b.ps("a_psT0", [128, 128], BF16)
    psT1 = b.ps("a_psT1", [128, 128], BF16)
    psA = b.ps("a_psA", [128, 512], F32)
    psB = b.ps("a_psB", [128, 512], F32)
    kv0 = b.ps("a_kv0", [1, 512], F32)
    kv1 = b.ps("a_kv1", [1, 512], F32)
    ks0 = b.ps("a_ks0", [1, 512], F32)
    ks1 = b.ps("a_ks1", [1, 512], F32)

    b.wait(te, C["s_init"])
    b.wait(ve, C["s_init"])
    b.wait(se, C["s_init"])

    for t in range(2 * nt if stages >= 1 else 0):
        src = d["x_own"] if t < nt else d["x_other"]
        row = (t % nt) * 128
        buf = t % 2
        xa = xt[:, buf * D:(buf + 1) * D]
        if t >= 2:
            b.wait(sy, s_fr, b.cnt[s_fr])
        b.inc(sy.dma_start(out=xa, in_=src[row:row + 128, :]), s_xl[buf], 16)
        b.wait(ve, s_xl[buf])
        v_ln = b.inc(ln_tile2(b, C, xnb[:], xa, scr, sv), s_tv, 1)
        dstT = xT_own[:, t * D:(t + 1) * D] if t < nt else xT_cur[:]
        if t >= nt:
            b.wait(ve, s_mm)
        for k in range(8):
            psT = (psT0, psT1)[k % 2]
            if k >= 2:
                b.wait(te, s_tv, b.cnt[s_tv])
            elif k == 0:
                b.wait(te, s_tv, v_ln)
            v = b.inc(te.transpose(psT[:], xnb[:, k * 128:(k + 1) * 128], C["eye_bf"][:]),
                      s_tp, 1)
            ve.wait_ge(s_tp, v)
            b.inc(ve.tensor_copy(dstT[:, k * 128:(k + 1) * 128], psT[:]), s_tv, 1)
        if t == 0:
            b.wait(te, s_wl)
        for half, pw in ((0, psA), (1, psB)):
            if t > 0:
                b.wait(te, s_ev, b.cnt[s_ev])
            for k in range(8):
                te.matmul(pw[:], dstT[:, k * 128:(k + 1) * 128],
                          Wk_sb[:, k * 1024 + half * 512:k * 1024 + half * 512 + 512],
                          start=(k == 0), stop=False)
            b.inc(te.matmul(pw[:], C["ones_r_bf"][:, 0:128],
                            bk_r[:, half * 512:half * 512 + 512],
                            start=False, stop=True), s_mm, 1)
        b.wait(ve, s_mm)
        ve.tensor_scalar(mn[:, 0:512], psA[:], 0.0, None, op0=ALU.min)
        va = b.inc(ve.tensor_scalar(mn[:, 512:1024], psB[:], 0.0, None, op0=ALU.min),
                   s_tv, 1)
        ve.tensor_scalar(Kf[:, 0:512], psA[:], 0.0, None, op0=ALU.max)
        vke = b.inc(ve.tensor_scalar(Kf[:, 512:1024], psB[:], 0.0, None, op0=ALU.max),
                    s_ev, 2)
        se.wait_ge(s_tv, va)
        vex = b.inc(se.activation(mn[:], mn[:], AF.Exp), s_ac, 1)
        for half, pw in ((0, psA), (1, psB)):
            b.wait(te, s_ev, vke)
            for k in range(8):
                te.matmul(pw[:], dstT[:, k * 128:(k + 1) * 128],
                          Wv_sb[:, k * 1024 + half * 512:k * 1024 + half * 512 + 512],
                          start=(k == 0), stop=False)
            b.inc(te.matmul(pw[:], C["ones_r_bf"][:, 0:128],
                            bv_r[:, half * 512:half * 512 + 512],
                            start=False, stop=True), s_mm, 1)
        ve.wait_ge(s_ac, vex)
        ve.tensor_add(Kf[:], Kf[:], mn[:])
        b.inc(ve.tensor_copy(Kb[:], Kf[:]), s_fr, 1)
        b.wait(ve, s_mm)
        ve.tensor_copy(Vf[:, 0:512], psA[:])
        vve = b.inc(ve.tensor_copy(Vf[:, 512:1024], psB[:]), s_ev, 2)
        vkv = b.inc(ve.tensor_mul(KVb[:], Kf[:], Vf[:]), s_tv, 1)
        te.wait_ge(s_tv, vkv)
        te.matmul(kv0[:], C["ones_c_bf"][:], KVb[:, 0:512], start=(t == 0), stop=(t == 2 * nt - 1))
        te.matmul(kv1[:], C["ones_c_bf"][:], KVb[:, 512:1024], start=(t == 0), stop=(t == 2 * nt - 1))
        te.matmul(ks0[:], C["ones_c_bf"][:], Kb[:, 0:512], start=(t == 0), stop=(t == 2 * nt - 1))
        v_acc = b.inc(te.matmul(ks1[:], C["ones_c_bf"][:], Kb[:, 512:1024],
                                start=(t == 0), stop=(t == 2 * nt - 1)), s_mm, 1)

    if stages < 2:
        b.wait(sy, s_st)
        return
    b.wait(ve, s_mm)
    ve.tensor_copy(kv_row[:, 0:512], kv0[:])
    ve.tensor_copy(kv_row[:, 512:1024], kv1[:])
    ve.tensor_copy(kv_row[:, 1024:1536], ks0[:])
    ve.tensor_copy(kv_row[:, 1536:2048], ks1[:])
    v_row = b.inc(ve.tensor_copy(kv_rowb[:], kv_row[:]), s_ev, 4)
    te.wait_ge(s_ev, v_row)
    for j, pw in ((0, psA), (1, psB)):
        inst = te.matmul(pw[:], C["ones_r_bf"][:, 0:128], kv_rowb[:, j * 512:(j + 1) * 512],
                         start=True, stop=True)
    b.inc(inst, s_mm, 1)
    b.wait(ve, s_mm)
    ve.tensor_copy(kv_bc[:, 0:512], psA[:])
    v_bc1 = b.inc(ve.tensor_copy(kv_bc[:, 512:1024], psB[:]), s_ev, 2)
    te.wait_ge(s_ev, v_bc1)
    for j, pw in ((2, psA), (3, psB)):
        inst = te.matmul(pw[:], C["ones_r_bf"][:, 0:128], kv_rowb[:, j * 512:(j + 1) * 512],
                         start=True, stop=True)
    b.inc(inst, s_mm, 1)
    b.wait(ve, s_mm)
    ve.tensor_copy(kv_bc[:, 1024:1536], psA[:])
    v_bc = b.inc(ve.tensor_copy(kv_bc[:, 1536:2048], psB[:]), s_ev, 2)

    if stages < 3:
        b.wait(sy, s_st)
        return
    for t in range(nt):
        row = t * 128
        buf = t % 2
        xa = xt[:, buf * D:(buf + 1) * D]
        b.wait(sy, s_fr, b.cnt[s_fr])
        b.inc(sy.dma_start(out=xa, in_=d["x_own"][row:row + 128, :]), s_xl[buf], 16)
        xTo = xT_own[:, t * D:(t + 1) * D]
        for half, pw in ((0, psA), (1, psB)):
            b.wait(te, s_ev, b.cnt[s_ev])
            for k in range(8):
                te.matmul(pw[:], xTo[:, k * 128:(k + 1) * 128],
                          Wq_sb[:, k * 1024 + half * 512:k * 1024 + half * 512 + 512],
                          start=(k == 0), stop=False)
            b.inc(te.matmul(pw[:], C["ones_r_bf"][:, 0:128],
                            bq_r[:, half * 512:half * 512 + 512],
                            start=False, stop=True), s_mm, 1)
        b.wait(ve, s_mm)
        ve.tensor_scalar(mn[:, 0:512], psA[:], 0.0, None, op0=ALU.min)
        vq1 = b.inc(ve.tensor_scalar(mn[:, 512:1024], psB[:], 0.0, None, op0=ALU.min),
                    s_tv, 1)
        ve.tensor_scalar(Qf[:, 0:512], psA[:], 0.0, None, op0=ALU.max)
        vqe = b.inc(ve.tensor_scalar(Qf[:, 512:1024], psB[:], 0.0, None, op0=ALU.max),
                    s_ev, 2)
        se.wait_ge(s_tv, vq1)
        vq2 = b.inc(se.activation(mn[:], mn[:], AF.Exp), s_ac, 1)
        ve.wait_ge(s_ac, vq2)
        ve.tensor_add(Qf[:], Qf[:], mn[:])
        if t == 0:
            ve.wait_ge(s_ev, v_bc)
        b.wait(ve, s_st)
        ve.tensor_mul(t1[:], Qf[:], kv_bc[:, 1024:2048])
        ve.tensor_scalar(t1[:], t1[:], 1e-6, None, op0=ALU.add)
        vrc = b.inc(ve.reciprocal(t1[:], t1[:]), s_tv, 1)
        ve.wait_ge(s_tv, vrc)
        ve.tensor_mul(t1[:], t1[:], kv_bc[:, 0:1024])
        v_vl = b.inc(ve.tensor_mul(xnb[:], t1[:], Qf[:]), s_tv, 1)
        b.wait(ve, s_mm)
        for k in range(8):
            psT = (psT0, psT1)[k % 2]
            b.wait(te, s_tv, b.cnt[s_tv] if k >= 2 else v_vl)
            v = b.inc(te.transpose(psT[:], xnb[:, k * 128:(k + 1) * 128], C["eye_bf"][:]),
                      s_tp, 1)
            ve.wait_ge(s_tp, v)
            b.inc(ve.tensor_copy(xT_cur[:, k * 128:(k + 1) * 128], psT[:]), s_tv, 1)
        for half, pw in ((0, psA), (1, psB)):
            b.wait(te, s_ev, b.cnt[s_ev])
            b.wait(te, s_tv)
            for k in range(8):
                te.matmul(pw[:], xT_cur[:, k * 128:(k + 1) * 128],
                          Wo_sb[:, k * 1024 + half * 512:k * 1024 + half * 512 + 512],
                          start=(k == 0), stop=False)
            b.inc(te.matmul(pw[:], C["ones_r_bf"][:, 0:128],
                            bo_r[:, half * 512:half * 512 + 512],
                            start=False, stop=True), s_mm, 1)
        b.wait(se, s_mm)
        se.activation(gsc[:, 0:512], psA[:], AF.Square)
        v_sq = b.inc(se.activation(gsc[:, 512:1024], psB[:], AF.Square), s_ac, 1)
        ve.wait_ge(s_ac, v_sq)
        ve.tensor_scalar(gsc[:], gsc[:], 0.035677408, 0.7978845608, op0=ALU.mult, op1=ALU.add)
        ve.tensor_mul(gsc[:, 0:512], gsc[:, 0:512], psA[:])
        v_u = b.inc(ve.tensor_mul(gsc[:, 512:1024], gsc[:, 512:1024], psB[:]), s_tv, 1)
        se.wait_ge(s_tv, v_u)
        v_th = b.inc(se.activation(gsc[:], gsc[:], AF.Tanh), s_ac, 1)
        ve.wait_ge(s_ac, v_th)
        ve.tensor_scalar(gsc[:], gsc[:], 1.0, 0.5, op0=ALU.add, op1=ALU.mult)
        ve.tensor_mul(at_out[:, 0:512], gsc[:, 0:512], psA[:])
        b.inc(ve.tensor_mul(at_out[:, 512:1024], gsc[:, 512:1024], psB[:]), s_ev, 2)
        ln_tile2(b, C, t1[:], at_out[:], scr, sv)
        b.wait(ve, s_xl[buf])
        v_o = b.inc(ve.tensor_add(t1[:], t1[:], xa), s_fr, 1)
        sy.wait_ge(s_fr, v_o)
        b.inc(sy.dma_start(out=d["resid1"][row:row + 128, :], in_=t1[:]), s_st, 16)
        if t == 0 and "dbg_q" in d:
            b.inc(sy.dma_start(out=d["dbg_q"], in_=Qf[:]), s_st, 16)
            b.inc(sy.dma_start(out=d["dbg_vl"], in_=xnb[:]), s_st, 16)
            b.inc(sy.dma_start(out=d["dbg_at"], in_=at_out[:]), s_st, 16)
            b.inc(sy.dma_start(out=d["dbg_kvbc"], in_=kv_bc[:]), s_st, 16)
            b.inc(sy.dma_start(out=d["dbg_xt"], in_=xT_own[:, 0:D]), s_st, 16)
            b.wait(ve, s_st)
            b.wait(te, s_st)

    b.wait(sy, s_st)



def emit_phase_f0(b, C, d):
    nc = b.nc
    te, ve, se, ge, sy = nc.tensor, nc.vector, nc.scalar, nc.gpsimd, nc.sync
    HID = 4096
    HC = 512
    NHC = HID // HC
    s_xl = [b.sem("f_xl0"), b.sem("f_xl1")]
    s_tp = b.sem("f_tp")
    s_tv = b.sem("f_tv")
    s_mm = b.sem("f_mm")
    s_me = b.sem("f_me")
    s_de = b.sem("f_de")
    s_ta = b.sem("f_ta")
    s_w1 = [b.sem("f_w1a"), b.sem("f_w1b")]
    s_w2 = [b.sem("f_w2a"), b.sem("f_w2b")]
    s_wf = b.sem("f_wf")
    s_bl = b.sem("f_bl")
    s_st = b.sem("f_st")
    s_fr = b.sem("f_fr")

    xeT = b.sb("f_xeT", [128, NT * D], BF16)
    w1_sb = [b.sb("f_w1A", [128, 8 * HC], BF16), b.sb("f_w1B", [128, 8 * HC], BF16)]
    w2_sb = [b.sb("f_w2A", [128, 4 * 1024], BF16), b.sb("f_w2B", [128, 4 * 1024], BF16)]
    h_sb = b.sb("f_h", [128, 4 * 512], BF16)
    acc = b.sb("f_acc", [128, NT * D])
    tmp = b.sb("f_tmp", [128, 2 * 1024])
    bias_sb = b.sb("f_bias", [1, 3 * HID + 3 * D], BF16)
    grow = b.sb("f_grow", [1, 2 * D])
    gv_bc = b.sb("f_gvbc", [128, D])
    msc = b.sb("f_msc", [128, D])
    scores = b.sb("f_scores", [128, 2 * NT])
    masks = b.sb("f_masks", [128, 2 * NT])
    xt = b.sb("f_xt", [128, 2 * D])
    scr = b.sb("f_scr", [128, D], BF16)
    sv = b.sb("f_sv", [128, 8])
    t1 = b.sb("f_t1", [128, D])
    xnb = b.sb("f_xnb", [128, D], BF16)

    psT0 = b.ps("f_psT0", [128, 128], BF16)
    psT1 = b.ps("f_psT1", [128, 128], BF16)
    psM0 = b.ps("f_psM0", [128, 512], F32)
    psM1 = b.ps("f_psM1", [128, 512], F32)
    psD0 = b.ps("f_psD0", [128, 1024], F32)

    b.wait(te, C["s_init"])
    b.wait(ve, C["s_init"])
    b.wait(se, C["s_init"])

    b.inc(ge.dma_start(out=grow[:, 0:D], in_=d["gate_w"][0:1, :]), s_bl, 16)
    b.inc(ge.dma_start(out=grow[:, D:2 * D], in_=d["gate_w"][1:2, :]), s_bl, 16)
    b.inc(ge.dma_start(out=bias_sb[:, 0:HID], in_=d["b1"][0:1, :]), s_bl, 16)
    b.inc(ge.dma_start(out=bias_sb[:, HID:2 * HID], in_=d["b1"][1:2, :]), s_bl, 16)
    b.inc(ge.dma_start(out=bias_sb[:, 2 * HID:3 * HID],
                       in_=d["sb1"].rearrange("(o n) -> o n", o=1)), s_bl, 16)
    b.inc(ge.dma_start(out=bias_sb[:, 3 * HID:3 * HID + D], in_=d["b2"][0:1, :]), s_bl, 16)
    b.inc(ge.dma_start(out=bias_sb[:, 3 * HID + D:3 * HID + 2 * D], in_=d["b2"][1:2, :]),
          s_bl, 16)
    b.inc(ge.dma_start(out=bias_sb[:, 3 * HID + 2 * D:3 * HID + 3 * D],
                       in_=d["sb2"].rearrange("(o n) -> o n", o=1)), s_bl, 16)
    b1r = [bias_sb[:, 2 * HID:3 * HID], bias_sb[:, 0:HID], bias_sb[:, HID:2 * HID]]
    b2r = [bias_sb[:, 3 * HID + 2 * D:3 * HID + 3 * D],
           bias_sb[:, 3 * HID:3 * HID + D], bias_sb[:, 3 * HID + D:3 * HID + 2 * D]]
    ge.memset(acc[:], 0.0)
    b.wait(te, s_bl)
    b.wait(ve, s_bl)
    v_gv = b.inc(ve.tensor_sub(grow[:, 0:D], grow[:, D:2 * D], grow[:, 0:D]), s_tv, 1)
    te.wait_ge(s_tv, v_gv)
    v_g1 = b.inc(te.matmul(psD0[:, 0:512], C["ones_r_f"][:, 0:128], grow[:, 0:512],
                           start=True, stop=True), s_mm, 1)
    v_g2 = b.inc(te.matmul(psD0[:, 512:1024], C["ones_r_f"][:, 0:128], grow[:, 512:1024],
                           start=True, stop=True), s_mm, 1)
    ve.wait_ge(s_mm, v_g2)
    v_gb = b.inc(ve.tensor_copy(gv_bc[:], psD0[:]), s_de, 1)
    te.wait_ge(s_de, v_gb)

    for t in range(NT):
        buf = t % 2
        xa = xt[:, buf * D:(buf + 1) * D]
        if t >= 2:
            b.wait(sy, s_fr, b.cnt[s_fr])
        b.inc(sy.dma_start(out=xa, in_=d["resid1"][t * 128:(t + 1) * 128, :]),
              s_xl[buf], 16)
        b.wait(ve, s_xl[buf])
        inst = ln_tile2(b, C, xnb[:], xa, scr, sv, out2_ap=msc[:])
        v_ln = b.inc(inst, s_tv, 1)
        ve.tensor_mul(msc[:], msc[:], gv_bc[:])
        vrd = b.inc(ve.reduce_sum(scores[:, t:t + 1], msc[:], axis=AX.X), s_tv, 1)
        ve.wait_ge(s_tv, vrd)
        b.inc(ve.tensor_copy(sv[:, 6:7], sv[:, 5:6]), s_fr, 1)
        for k in range(8):
            psT = (psT0, psT1)[k % 2]
            b.wait(te, s_tv, b.cnt[s_tv] if k >= 2 else v_ln)
            v = b.inc(te.transpose(psT[:], xnb[:, k * 128:(k + 1) * 128], C["eye_bf"][:]),
                      s_tp, 1)
            ve.wait_ge(s_tp, v)
            b.inc(ve.tensor_copy(xeT[:, k * NTOK + t * 128:k * NTOK + (t + 1) * 128],
                                 psT[:]), s_tv, 1)

    va = b.inc(ve.tensor_copy(scores[:, NT:2 * NT], scores[:, 0:NT]), s_tv, 1)
    se.wait_ge(s_tv, va)
    vb = b.inc(se.sign(scores[:, 0:NT], scores[:, NT:2 * NT]), s_me, 1)
    ve.wait_ge(s_me, vb)
    vmx = b.inc(ve.tensor_scalar(masks[:, NT:2 * NT], scores[:, 0:NT], 0.0, None,
                                 op0=ALU.max), s_tv, 1)
    ve.wait_ge(s_tv, vmx)
    v_mk = b.inc(ve.tensor_scalar(masks[:, 0:NT], masks[:, NT:2 * NT], -1.0, 1.0,
                                  op0=ALU.mult, op1=ALU.add), s_tv, 1)
    ve.wait_ge(s_tv, v_mk)
    if "masks_in" in d:
        b.inc(sy.dma_start(out=masks[:], in_=d["masks_in"]), s_bl, 16)
        b.wait(ve, s_bl)
        v_mk = b.inc(ve.tensor_copy(scores[:, 0:NT], masks[:, 0:NT]), s_tv, 1)

    passes = [("sw1", "sw2", None, None), ("w1", "w2", 0, 0), ("w1", "w2", 1, 1)]
    widx = 0
    for pi, (w1n, w2n, ei, mi) in enumerate(passes):
        if mi is not None:
            b.wait(ve, s_mm)
            for t in range(NT):
                mcol = masks[:, mi * NT + t:mi * NT + t + 1]
                buf = t % 2
                xa = xt[:, buf * D:(buf + 1) * D]
                b.wait(sy, s_fr, b.cnt[s_fr])
                b.inc(sy.dma_start(out=xa, in_=d["resid1"][t * 128:(t + 1) * 128, :]),
                      s_xl[buf], 16)
                b.wait(ve, s_xl[buf])
                if t == 0:
                    ve.wait_ge(s_tv, v_mk)
                ln_tile2(b, C, msc[:], xa, scr, sv)
                inst2 = ve.tensor_scalar(xnb[:], msc[:], mcol, None, op0=ALU.mult)
                v_xe = b.inc(inst2, s_tv, 1)
                b.inc(ve.tensor_copy(sv[:, 6:7], sv[:, 5:6]), s_fr, 1)
                for k in range(8):
                    psT = (psT0, psT1)[k % 2]
                    b.wait(te, s_tv, b.cnt[s_tv] if k >= 2 else v_xe)
                    v = b.inc(te.transpose(psT[:], xnb[:, k * 128:(k + 1) * 128],
                                           C["eye_bf"][:]), s_tp, 1)
                    ve.wait_ge(s_tp, v)
                    b.inc(ve.tensor_copy(
                        xeT[:, k * NTOK + t * 128:k * NTOK + (t + 1) * 128], psT[:]),
                        s_tv, 1)
        w1ap = (d["w1"][ei] if w1n == "w1" else d["sw1"])
        w2ap = (d["w2"][ei] if w2n == "w2" else d["sw2"])
        b.wait(te, s_tv)
        if pi == 1 and "dbg_xeT" in d:
            b.wait(sy, s_tv)
            vdmp = b.inc(sy.dma_start(out=d["dbg_xeT"], in_=xeT[:]), s_st, 16)
            b.wait(te, s_st)
            b.wait(ve, s_st)
        for hc in range(NHC):
            wbuf = widx % 2
            if widx >= 2:
                ge.wait_ge(s_wf, widx - 1)
            b.inc(ge.dma_start(out=w1_sb[wbuf][:].rearrange("p (kc n) -> p kc n", kc=8),
                               in_=w1ap[:, hc * HC:(hc + 1) * HC]
                               .rearrange("(kc p) n -> p kc n", p=128)),
                  s_w1[wbuf], 16)
            b.inc(ge.dma_start(out=w2_sb[wbuf][:].rearrange("p (kc n) -> p kc n", kc=4),
                               in_=w2ap[hc * HC:(hc + 1) * HC, :]
                               .rearrange("(kc p) n -> p kc n", p=128)),
                  s_w2[wbuf], 16)
            b.wait(te, s_w1[wbuf])
            b.wait(te, s_w2[wbuf])
            b1s = b1r[pi][:, hc * HC:(hc + 1) * HC]
            last_down = None
            for tb in range(4):
                for i in range(4):
                    psM = (psM0, psM1)[i % 2]
                    if b.cnt[s_me] >= 2:
                        b.wait(te, s_me, b.cnt[s_me] - 1)
                    for k in range(8):
                        te.matmul(psM[:],
                                  w1_sb[wbuf][:, k * HC + i * 128:k * HC + (i + 1) * 128],
                                  xeT[:, k * NTOK + tb * 512:k * NTOK + (tb + 1) * 512],
                                  start=(k == 0), stop=False)
                    v = b.inc(te.matmul(psM[:], b1s[:, i * 128:(i + 1) * 128],
                                        C["ones_r_bf"][:, 0:512], start=False, stop=True),
                              s_mm, 1)
                    se.wait_ge(s_mm, v)
                    b.inc(se.activation(h_sb[:, i * 512:(i + 1) * 512], psM[:], AF.Relu),
                          s_me, 1)
                b.wait(te, s_me)
                for st2 in range(4):
                    srow = tb * 512 + st2 * 128
                    tile_i = srow // 128
                    psD = psD0
                    if b.cnt[s_de] >= 1:
                        b.wait(te, s_de, b.cnt[s_de])
                    for half in range(2):
                        pslice = psD[:, half * 512:(half + 1) * 512]
                        first = True
                        if hc == 0:
                            te.matmul(pslice, C["ones_r_bf"][:, 0:128],
                                      b2r[pi][:, half * 512:(half + 1) * 512],
                                      start=True, stop=False)
                            first = False
                        for i in range(4):
                            last_down = te.matmul(
                                pslice,
                                h_sb[:, i * 512 + st2 * 128:i * 512 + (st2 + 1) * 128],
                                w2_sb[wbuf][:, i * 1024 + half * 512:
                                            i * 1024 + (half + 1) * 512],
                                start=first, stop=(i == 3))
                            first = False
                    v = b.inc(last_down, s_mm, 1)
                    tbuf = st2 % 2
                    ve.wait_ge(s_mm, v)
                    if b.cnt[s_ta] > 0:
                        b.wait(ve, s_ta, b.cnt[s_ta] - 1)
                    vv = b.inc(ve.tensor_copy(tmp[:, tbuf * 1024:(tbuf + 1) * 1024],
                                              psD[:]), s_de, 1)
                    ge.wait_ge(s_de, vv)
                    b.inc(ge.tensor_tensor(acc[:, tile_i * 1024:(tile_i + 1) * 1024],
                                           acc[:, tile_i * 1024:(tile_i + 1) * 1024],
                                           tmp[:, tbuf * 1024:(tbuf + 1) * 1024],
                                           op=ALU.add), s_ta, 1)
            b.inc(last_down, s_wf, 1) if False else None
            b.inc(ve.tensor_copy(sv[:, 7:8], sv[:, 5:6]), s_wf, 1)
            widx += 1

    b.wait(ve, s_ta)
    if "dbg_acc" in d:
        b.wait(sy, s_ta)
        b.inc(sy.dma_start(out=d["dbg_acc"], in_=acc[:]), s_st, 16)
        b.inc(sy.dma_start(out=d["dbg_masks"], in_=masks[:]), s_st, 16)
    for t in range(NT):
        buf = t % 2
        xa = xt[:, buf * D:(buf + 1) * D]
        b.wait(sy, s_fr, b.cnt[s_fr])
        b.inc(sy.dma_start(out=xa, in_=d["resid1"][t * 128:(t + 1) * 128, :]),
              s_xl[buf], 16)
        b.wait(ve, s_st)
        ln_tile2(b, C, t1[:], acc[:, t * D:(t + 1) * D], scr, sv)
        b.wait(ve, s_xl[buf])
        v_o = b.inc(ve.tensor_add(t1[:], t1[:], xa), s_fr, 1)
        sy.wait_ge(s_fr, v_o)
        b.inc(sy.dma_start(out=d["resid2"][t * 128:(t + 1) * 128, :], in_=t1[:]),
              s_st, 16)
    b.wait(sy, s_st)


def emit_phase_f1(b, C, d, npasses=9, sub=3):
    nc = b.nc
    te, ve, se, ge, sy = nc.tensor, nc.vector, nc.scalar, nc.gpsimd, nc.sync
    HID = 2048
    HC = 512
    NHC = HID // HC
    NE = 8
    s_xl = [b.sem("g_xl0"), b.sem("g_xl1")]
    s_tp = b.sem("g_tp")
    s_tv = b.sem("g_tv")
    s_mm = b.sem("g_mm")
    s_me = b.sem("g_me")
    s_de = b.sem("g_de")
    s_ta = b.sem("g_ta")
    s_wg = [b.sem("g_wga"), b.sem("g_wgb")]
    s_wu = [b.sem("g_wua"), b.sem("g_wub")]
    s_wd = [b.sem("g_wda"), b.sem("g_wdb")]
    s_wf = b.sem("g_wf")
    s_bl = b.sem("g_bl")
    s_st = b.sem("g_st")
    s_fr = b.sem("g_fr")
    s_na = b.sem("g_na")
    s_nb = b.sem("g_nb")

    xfwT = b.sb("g_xfwT", [128, NT * D], BF16)
    wg_sb = [b.sb("g_wgA", [128, 8 * HC], BF16), b.sb("g_wgB", [128, 8 * HC], BF16)]
    wu_sb = [b.sb("g_wuA", [128, 8 * HC], BF16), b.sb("g_wuB", [128, 8 * HC], BF16)]
    wd_one = b.sb("g_wdA", [128, 4 * 1024], BF16)
    wd_sb = [wd_one, wd_one]
    s_wd_done = b.sem("g_wdd")
    h_sb = b.sb("g_h", [128, 4 * NTOK], BF16)
    acc = b.sb("g_acc", [128, NT * D])
    tmp = b.sb("g_tmp", [128, 1024])
    mt = tmp
    W_bc = b.sb("g_wbc", [128, NTOK])
    lnw_bc = b.sb("g_lnwbc", [128, D])
    lnb_bc = b.sb("g_lnbbc", [128, D])
    rows = b.sb("g_rows", [1, 2 * D])
    router_sb = b.sb("g_router", [128, 8 * 8])
    masks_w = b.sb("g_masksw", [128, NT * NE])
    w_row = b.sb("g_wrow", [1, NTOK])
    lg = b.sb("g_lg", [128, 16])
    xfT_f = b.sb("g_xfTf", [128, D])
    xt = b.sb("g_xt", [128, D])
    scr = b.sb("g_scr", [128, D], BF16)
    sv = b.sb("g_sv", [128, 8])
    msc = b.sb("g_msc", [128, D])
    t1 = xfT_f

    psT0 = b.ps("g_psT0", [128, 128], F32)
    psT1 = b.ps("g_psT1", [128, 128], F32)
    psL = b.ps("g_psL", [128, 8], F32)
    psG = b.ps("g_psG", [128, 512], F32)
    psU = b.ps("g_psU", [128, 512], F32)
    psD = b.ps("g_psD", [128, 1024], F32)

    b.wait(te, C["s_init"])
    b.wait(ve, C["s_init"])
    b.wait(se, C["s_init"])

    b.inc(ge.dma_start(out=rows[:, 0:D], in_=d["ln_w"].rearrange("(o n) -> o n", o=1)),
          s_bl, 16)
    b.inc(ge.dma_start(out=rows[:, D:2 * D], in_=d["ln_b"].rearrange("(o n) -> o n", o=1)),
          s_bl, 16)
    b.inc(ge.dma_start(out=router_sb[:].rearrange("p (kc e) -> p kc e", kc=8),
                       in_=d["router_w"].rearrange("(kc p) e -> p kc e", p=128)),
          s_bl, 16)
    ge.memset(acc[:], 0.0)
    b.wait(te, s_bl)
    for j, dst in ((0, lnw_bc), (1, lnb_bc)):
        for hh in range(2):
            b.wait(te, s_de, b.cnt[s_de])
            v = b.inc(te.matmul(psD[:, hh * 512:(hh + 1) * 512], C["ones_r_f"][:, 0:128],
                                rows[:, j * D + hh * 512:j * D + (hh + 1) * 512],
                                start=True, stop=True), s_mm, 1)
            ve.wait_ge(s_mm, v)
            b.inc(ve.tensor_copy(dst[:, hh * 512:(hh + 1) * 512],
                                 psD[:, hh * 512:(hh + 1) * 512]), s_de, 1)
    b.wait(te, s_de)

    for t in range(NT):
        buf = 0
        xa = xt[:, 0:D]
        b.wait(sy, s_fr, b.cnt[s_fr])
        b.inc(sy.dma_start(out=xa, in_=d["resid3"][t * 128:(t + 1) * 128, :]),
              s_xl[buf], 16)
        b.wait(ve, s_xl[buf])
        ln_tile2(b, C, msc[:], xa, scr, sv)
        ve.tensor_mul(msc[:], msc[:], lnw_bc[:])
        v_ln = b.inc(ve.tensor_add(msc[:], msc[:], lnb_bc[:]), s_tv, 1)
        b.inc(ve.tensor_copy(sv[:, 6:7], sv[:, 5:6]), s_fr, 1)
        for k in range(8):
            psT = (psT0, psT1)[k % 2]
            b.wait(te, s_tv, b.cnt[s_tv] if k >= 2 else v_ln)
            v = b.inc(te.transpose(psT[:], msc[:, k * 128:(k + 1) * 128], C["eye_f"][:]),
                      s_tp, 1)
            ve.wait_ge(s_tp, v)
            ve.tensor_copy(xfT_f[:, k * 128:(k + 1) * 128], psT[:])
            b.inc(ve.tensor_copy(xfwT[:, k * NTOK + t * 128:k * NTOK + (t + 1) * 128],
                                 psT[:]), s_tv, 1)
        b.wait(te, s_tv)
        if t > 0:
            b.wait(te, s_de, b.cnt[s_de])
        vs_ = None
        for k in range(8):
            vs_ = te.matmul(psL[:], xfT_f[:, k * 128:(k + 1) * 128],
                            router_sb[:, k * 8:(k + 1) * 8], start=(k == 0), stop=(k == 7))
        v = b.inc(vs_, s_mm, 1)
        ve.wait_ge(s_mm, v)
        v1 = b.inc(ve.tensor_copy(lg[:, 0:8], psL[:]), s_de, 1)
        ve.wait_ge(s_de, v1)
        v2 = b.inc(ve.reduce_max(sv[:, 7:8], lg[:, 0:8], axis=AX.X), s_na, 1)
        ve.wait_ge(s_na, v2)
        v3 = b.inc(ve.tensor_scalar(lg[:, 8:16], lg[:, 0:8], sv[:, 7:8], None,
                                    op0=ALU.subtract), s_na, 1)
        se.wait_ge(s_na, v3)
        v4 = b.inc(se.sign(lg[:, 0:8], lg[:, 8:16]), s_nb, 1)
        v5 = b.inc(se.activation(sv[:, 6:7], sv[:, 7:8], AF.Sigmoid), s_nb, 1)
        ve.wait_ge(s_nb, v5)
        v6 = b.inc(ve.tensor_scalar(lg[:, 8:16], lg[:, 0:8], 1.0, None, op0=ALU.add),
                   s_na, 1)
        ve.wait_ge(s_na, v6)
        v7 = b.inc(ve.tensor_scalar(masks_w[:, t * NE:(t + 1) * NE], lg[:, 8:16],
                                    sv[:, 6:7], None, op0=ALU.mult), s_na, 1)
        ve.wait_ge(s_na, v7)


    widx = 0
    for pi in range(npasses):
        ei = pi - 1
        if ei >= 0:
            for t in range(NT):
                psT = (psT0, psT1)[t % 2]
                if t >= 2:
                    b.wait(te, s_de, b.cnt[s_de])
                v = b.inc(te.transpose(psT[0:1, 0:128],
                                       masks_w[:, t * NE + ei:t * NE + ei + 1],
                                       C["eye_f"][:]), s_tp, 1)
                ve.wait_ge(s_tp, v)
                b.inc(ve.tensor_copy(w_row[:, t * 128:(t + 1) * 128], psT[0:1, 0:128]),
                      s_de, 1)
            b.wait(te, s_de)
            for tb in range(4):
                b.wait(te, s_de, b.cnt[s_de])
                v = b.inc(te.matmul(psD[:, 0:512], C["ones_r_f"][:, 0:128],
                                    w_row[:, tb * 512:(tb + 1) * 512],
                                    start=True, stop=True), s_mm, 1)
                ve.wait_ge(s_mm, v)
                b.inc(ve.tensor_copy(W_bc[:, tb * 512:(tb + 1) * 512], psD[:, 0:512]),
                      s_de, 1)
            b.wait(te, s_de)
        gap = d["gate_w"][ei] if ei >= 0 else d["sh_gate"]
        uap = d["up_w"][ei] if ei >= 0 else d["sh_up"]
        dap = d["down_w"][ei] if ei >= 0 else d["sh_down"]
        for hc in range(NHC):
            wbuf = widx % 2
            if widx >= 2:
                ge.wait_ge(s_wf, widx - 1)
            b.inc(ge.dma_start(out=wg_sb[wbuf][:].rearrange("p (kc n) -> p kc n", kc=8),
                               in_=gap[:, hc * HC:(hc + 1) * HC]
                               .rearrange("(kc p) n -> p kc n", p=128)), s_wg[wbuf], 16)
            b.inc(ge.dma_start(out=wu_sb[wbuf][:].rearrange("p (kc n) -> p kc n", kc=8),
                               in_=uap[:, hc * HC:(hc + 1) * HC]
                               .rearrange("(kc p) n -> p kc n", p=128)), s_wu[wbuf], 16)
            if widx >= 1:
                b.wait(ge, s_wd_done)
            b.inc(ge.dma_start(out=wd_sb[wbuf][:].rearrange("p (kc n) -> p kc n", kc=4),
                               in_=dap[hc * HC:(hc + 1) * HC, :]
                               .rearrange("(kc p) n -> p kc n", p=128)), s_wd[wbuf], 16)
            b.wait(te, s_wg[wbuf])
            b.wait(te, s_wu[wbuf])
            b.wait(te, s_wd[wbuf])
            for i in range(4 if sub >= 2 else 0):
                for tb in range(4):
                    if b.cnt[s_me] >= 1:
                        b.wait(te, s_me, b.cnt[s_me])
                    for k in range(8):
                        te.matmul(psG[:],
                                  wg_sb[wbuf][:, k * HC + i * 128:k * HC + (i + 1) * 128],
                                  xfwT[:, k * NTOK + tb * 512:k * NTOK + (tb + 1) * 512],
                                  start=(k == 0), stop=(k == 7))
                    vg = None
                    for k in range(8):
                        vg = te.matmul(psU[:],
                                       wu_sb[wbuf][:, k * HC + i * 128:k * HC + (i + 1) * 128],
                                       xfwT[:, k * NTOK + tb * 512:k * NTOK + (tb + 1) * 512],
                                       start=(k == 0), stop=(k == 7))
                    v = b.inc(vg, s_mm, 1)
                    ve.wait_ge(s_mm, v)
                    if ei >= 0:
                        vm = b.inc(ve.tensor_tensor(mt[:, 0:512], psG[:],
                                                    W_bc[:, tb * 512:(tb + 1) * 512],
                                                    op=ALU.mult), s_tv, 1)
                        se.wait_ge(s_tv, vm)
                        vsl = b.inc(se.activation(mt[:, 0:512], mt[:, 0:512], AF.Silu),
                                    s_nb, 1)
                    else:
                        se.wait_ge(s_mm, v)
                        vsl = b.inc(se.activation(mt[:, 0:512], psG[:], AF.Silu), s_nb, 1)
                    ve.wait_ge(s_nb, vsl)
                    b.inc(ve.tensor_tensor(
                        h_sb[:, i * NTOK + tb * 512:i * NTOK + (tb + 1) * 512],
                        mt[:, 0:512], psU[:], op=ALU.mult), s_me, 1)
            b.wait(te, s_me)
            for st2 in range(NT if sub >= 3 else 0):
                psDl = psD
                if b.cnt[s_de] >= 1:
                    b.wait(te, s_de, b.cnt[s_de])
                for half in range(2):
                    vg = None
                    for k in range(4):
                        vg = te.matmul(psDl[:, half * 512:(half + 1) * 512],
                                       h_sb[:, k * NTOK + st2 * 128:k * NTOK + (st2 + 1) * 128],
                                       wd_sb[wbuf][:, k * 1024 + half * 512:
                                                   k * 1024 + (half + 1) * 512],
                                       start=(k == 0), stop=(k == 3))
                v = b.inc(vg, s_mm, 1)
                ve.wait_ge(s_mm, v)
                if b.cnt[s_ta] > 0:
                    b.wait(ve, s_ta, b.cnt[s_ta] - 1)
                if ei >= 0:
                    vv = b.inc(ve.tensor_scalar(tmp[:], psDl[:],
                                                masks_w[:, st2 * NE + ei:st2 * NE + ei + 1],
                                                None, op0=ALU.mult), s_de, 1)
                else:
                    vv = b.inc(ve.tensor_copy(tmp[:], psDl[:]), s_de, 1)
                ge.wait_ge(s_de, vv)
                b.inc(ge.tensor_tensor(acc[:, st2 * 1024:(st2 + 1) * 1024],
                                       acc[:, st2 * 1024:(st2 + 1) * 1024],
                                       tmp[:], op=ALU.add), s_ta, 1)
            b.inc(ve.tensor_copy(sv[:, 7:8], sv[:, 5:6]), s_wf, 1)
            b.inc(ve.tensor_copy(sv[:, 7:8], sv[:, 5:6]), s_wd_done, 1)
            widx += 1

    b.wait(ve, s_ta)
    for t in range(NT):
        buf = 0
        xa = xt[:, 0:D]
        b.wait(sy, s_fr, b.cnt[s_fr])
        b.inc(sy.dma_start(out=xa, in_=d["resid3"][t * 128:(t + 1) * 128, :]),
              s_xl[buf], 16)
        b.wait(ve, s_st)
        b.wait(ve, s_xl[buf])
        ln_tile2(b, C, msc[:], xa, scr, sv)
        ve.tensor_add(msc[:], msc[:], acc[:, t * D:(t + 1) * D])
        ln_tile2(b, C, t1[:], msc[:], scr, sv)
        v_o = b.inc(ve.tensor_add(t1[:], t1[:], xa), s_fr, 1)
        sy.wait_ge(s_fr, v_o)
        b.inc(sy.dma_start(out=d["out"][t * 128:(t + 1) * 128, :], in_=t1[:]),
              s_st, 16)
    b.wait(sy, s_st)


def emit_phase_m(b, C, d):
    nc = b.nc
    te, ve, se, ge, sy = nc.tensor, nc.vector, nc.scalar, nc.gpsimd, nc.sync
    DI = 2048
    NH = 32
    HP = 64
    DS = 128
    NOC = 18
    s_l = b.sem("m_l")
    s_lx = b.sem("m_lx")
    s_t = b.sem("m_t")
    s_v = b.sem("m_v")
    s_a = b.sem("m_a")
    s_p = b.sem("m_p")
    s_st = b.sem("m_st")

    Wx_sb = b.sb("m_Wx", [128, 8 * 2304], BF16)
    Wd_sb = b.sb("m_Wd", [128, 8 * 32], BF16)
    Wz_sb = b.sb("m_Wz", [128, 8 * 2048], BF16)
    Wo_sb = b.sb("m_Wo", [128, 16 * 1024], BF16)
    cw = b.sb("m_cw", [128, NOC * 4])
    cb = b.sb("m_cb", [128, NOC])
    dtb_r = b.sb("m_dtbr", [1, 32], BF16)
    nw_bc = b.sb("m_nwbc", [128, DI])
    Arow = b.sb("m_Arow", [128, 32])
    pv = b.sb("m_pv", [128, 1])
    rows = b.sb("m_rows", [1, 4 * 32])

    xt = b.sb("m_xt", [128, D])
    scr = b.sb("m_scr", [128, DI], BF16)
    sv = b.sb("m_sv", [128, 8])
    msc = b.sb("m_msc", [128, DI])
    xnb = b.sb("m_xnb", [128, D], BF16)
    uT = b.sb("m_uT", [128, D], BF16)
    xcat = b.sb("m_xcat", [128, NOC * 131])
    xbc = b.sb("m_xbc", [128, NOC * 128])
    Xp = b.sb("m_Xp", [128, DI])
    Btm = b.sb("m_Btm", [128, DS])
    dts = b.sb("m_dts", [128, 8 * 32])
    R = b.sb("m_R", [128, DI])
    Ysb = b.sb("m_Ysb", [128, DI])
    Gt = b.sb("m_Gt", [128, 128])
    Mt = b.sb("m_Mt", [128, 128])
    zs = b.sb("m_zs", [128, DI])
    ygb = b.sb("m_ygb", [128, DI], BF16)
    ygT = b.sb("m_ygT", [128, DI], BF16)
    ao = None

    psY = b.ps("m_psY", [128, DI], F32)
    psT = b.ps("m_psT", [128, 128], F32)
    psTb = b.ps("m_psTb", [128, 128], BF16)
    psX = b.ps("m_psX", [128, 256], F32)
    psD2 = b.ps("m_psD2", [128, 512], F32)

    b.wait(te, C["s_init"])
    b.wait(ve, C["s_init"])
    b.wait(se, C["s_init"])

    b.inc(ge.dma_start(out=Wx_sb[:].rearrange("p (kc n) -> p kc n", kc=8),
                       in_=d["W_in"][:, DI:DI + 2304]
                       .rearrange("(kc p) n -> p kc n", p=128)), s_l, 16)
    b.inc(ge.dma_start(out=Wd_sb[:].rearrange("p (kc n) -> p kc n", kc=8),
                       in_=d["W_in"][:, DI + 2304:DI + 2336]
                       .rearrange("(kc p) n -> p kc n", p=128)), s_l, 16)
    b.inc(ge.dma_start(out=Wz_sb[:].rearrange("p (kc n) -> p kc n", kc=8),
                       in_=d["W_in"][:, 0:DI]
                       .rearrange("(kc p) n -> p kc n", p=128)), s_l, 16)
    b.inc(ge.dma_start(out=Wo_sb[:].rearrange("p (kc n) -> p kc n", kc=16),
                       in_=d["W_out"].rearrange("(kc p) n -> p kc n", p=128)), s_l, 16)
    b.inc(ge.dma_start(out=cw[:].rearrange("p (oc j) -> p oc j", j=4),
                       in_=d["conv_w"].rearrange("(oc p) j -> p oc j", p=128)), s_l, 16)
    b.inc(ge.dma_start(out=cb[:], in_=d["conv_b"].rearrange("(oc p) -> p oc", p=128),
                       allow_slow_non_contiguous=True), s_l, 16)
    b.inc(ge.dma_start(out=dtb_r[:], in_=d["dt_bias"].rearrange("(o n) -> o n", o=1)),
          s_l, 16)
    b.inc(ge.dma_start(out=rows[:, 0:32], in_=d["A_log"].rearrange("(o n) -> o n", o=1)),
          s_l, 16)
    b.inc(ge.dma_start(out=pv[:], in_=d["prev_valid"]), s_l, 16)
    b.inc(ge.dma_start(out=rows[:, 32:64],
                       in_=d["norm_w"].rearrange("(o n) -> o n", o=1),
                       allow_slow_non_contiguous=True), s_l, 16) if False else None
    nwrow = b.sb("m_nwrow", [1, DI])
    b.inc(ge.dma_start(out=nwrow[:], in_=d["norm_w"].rearrange("(o n) -> o n", o=1)),
          s_l, 16)
    ge.memset(R[:], 0.0)
    ge.memset(xcat[:], 0.0)
    b.wait(se, s_l)
    va = b.inc(se.activation(rows[:, 64:96], rows[:, 0:32], AF.Exp), s_a, 1)
    ve.wait_ge(s_a, va)
    vb = b.inc(ve.tensor_scalar(rows[:, 96:128], rows[:, 64:96], -1.0, None,
                                op0=ALU.mult), s_v, 1)
    te.wait_ge(s_v, vb)
    v = b.inc(te.matmul(psT[:, 0:32], C["ones_r_f"][:, 0:128], rows[:, 96:128],
                        start=True, stop=True), s_t, 1)
    ve.wait_ge(s_t, v)
    v_ar = b.inc(ve.tensor_copy(Arow[:], psT[:, 0:32]), s_v, 1)
    for hh in range(4):
        b.wait(te, s_v, b.cnt[s_v])
        v = b.inc(te.matmul(psD2[:], C["ones_r_f"][:, 0:128],
                            nwrow[:, hh * 512:(hh + 1) * 512], start=True, stop=True),
                  s_t, 1)
        ve.wait_ge(s_t, v)
        b.inc(ve.tensor_copy(nw_bc[:, hh * 512:(hh + 1) * 512], psD2[:]), s_v, 1)
    b.wait(te, s_v)
    b.wait(te, s_l)

    dt_c = dts[:, 0:32]
    asc_c = dts[:, 32:64]
    acs_c = dts[:, 64:96]
    ds_c = dts[:, 96:128]
    od_c = dts[:, 128:160]
    E_c = dts[:, 160:192]

    for c in range(32):
        own = c >= 16
        src = d["resid2_prev"] if not own else d["resid2_own"]
        row = (c % 16) * 128
        b.wait(sy, s_v, b.cnt[s_v])
        b.inc(sy.dma_start(out=xt[:], in_=src[row:row + 128, :]), s_lx, 16)
        b.wait(ve, s_lx)
        b.wait(ve, s_l)
        b.wait(ve, s_st)
        inst = ln_tile2(b, C, msc[:, 0:D], xt[:], scr, sv, width=D)
        ve.tensor_copy(xnb[:, 0:D], msc[:, 0:D])
        v_ln = b.inc(ve.tensor_copy(xnb[:, 0:D], msc[:, 0:D]), s_v, 1)
        for k in range(8):
            b.wait(te, s_v, b.cnt[s_v] if k >= 1 else v_ln)
            v = b.inc(te.transpose(psTb[:], xnb[:, k * 128:(k + 1) * 128], C["eye_bf"][:]),
                      s_p, 1)
            ve.wait_ge(s_p, v)
            b.inc(ve.tensor_copy(uT[:, k * 128:(k + 1) * 128], psTb[:]), s_v, 1)
        b.wait(te, s_v)
        for og in range(9):
            if og >= 1:
                b.wait(te, s_v, b.cnt[s_v])
            for sub in range(2):
                oc = og * 2 + sub
                vs_ = None
                for k in range(8):
                    vs_ = te.matmul(psX[:, sub * 128:(sub + 1) * 128],
                                    Wx_sb[:, k * 2304 + oc * 128:k * 2304 + (oc + 1) * 128],
                                    uT[:, k * 128:(k + 1) * 128],
                                    start=(k == 0), stop=(k == 7))
            v = b.inc(vs_, s_t, 1)
            ve.wait_ge(s_t, v)
            for sub in range(2):
                oc = og * 2 + sub
                b.inc(ve.tensor_copy(xcat[:, oc * 131 + 3:oc * 131 + 131],
                                     psX[:, sub * 128:(sub + 1) * 128]), s_v, 1)
        b.wait(te, s_v, b.cnt[s_v])
        vs_ = None
        for k in range(8):
            vs_ = te.matmul(psD2[:, 32:64], uT[:, k * 128:(k + 1) * 128],
                            Wd_sb[:, k * 32:(k + 1) * 32], start=(k == 0), stop=False)
        v = b.inc(te.matmul(psD2[:, 32:64], C["ones_r_bf"][:, 0:128], dtb_r[:],
                            start=False, stop=True), s_t, 1)
        ve.wait_ge(s_t, v)
        v1 = b.inc(ve.tensor_scalar(dts[:, 192:224], psD2[:, 32:64], 0.0, None,
                                    op0=ALU.max), s_v, 1)
        se.wait_ge(s_v, v1)
        v2 = b.inc(se.activation(dts[:, 224:256], psD2[:, 32:64], AF.Abs), s_a, 1)
        se.wait_ge(s_a, v2)
        v3 = b.inc(se.activation(dts[:, 224:256], dts[:, 224:256], AF.Exp, scale=-1.0),
                   s_a, 1)
        se.wait_ge(s_a, v3)
        v4 = b.inc(se.activation(dts[:, 224:256], dts[:, 224:256], AF.Ln, bias=1.0),
                   s_a, 1)
        ve.wait_ge(s_a, v4)
        v5 = b.inc(ve.tensor_add(dt_c, dts[:, 192:224], dts[:, 224:256]), s_v, 1)
        ve.wait_ge(s_v, v5)
        v6 = b.inc(ve.tensor_mul(asc_c, dt_c, Arow[:]), s_v, 1)
        te.wait_ge(s_v, v6)
        v7 = b.inc(te.matmul(psD2[:, 64:96], C["triu_f"][:], asc_c, start=True,
                             stop=True), s_t, 1)
        ve.wait_ge(s_t, v7)
        v8 = b.inc(ve.tensor_copy(acs_c, psD2[:, 64:96]), s_v, 1)
        ve.wait_ge(s_v, v8)
        te.wait_ge(s_v, v8)
        v9 = b.inc(te.matmul(psD2[0:1, 96:128], C["ones_f"][:, 0:1], asc_c,
                             start=True, stop=True), s_t, 1)
        ve.wait_ge(s_t, v9)
        v10 = b.inc(ve.tensor_copy(rows[:, 96:128], psD2[0:1, 96:128]), s_v, 1)
        te.wait_ge(s_v, v10)
        v11 = b.inc(te.matmul(psD2[:, 96:128], C["ones_r_f"][:, 0:128],
                              rows[:, 96:128], start=True, stop=True), s_t, 1)
        ve.wait_ge(s_t, v11)
        v12 = b.inc(ve.tensor_sub(ds_c, psD2[:, 96:128], acs_c), s_v, 1)
        se.wait_ge(s_v, v12)
        v13 = b.inc(se.activation(ds_c, ds_c, AF.Exp), s_a, 1)
        v14 = b.inc(se.activation(od_c, acs_c, AF.Exp), s_a, 1)
        ve.wait_ge(s_a, v14)
        v15 = b.inc(ve.tensor_copy(E_c, psD2[:, 96:128]), s_v, 1)
        se.wait_ge(s_v, v15)
        v16 = b.inc(se.activation(E_c, E_c, AF.Exp), s_a, 1)
        b.wait(ve, s_v)
        for oc in range(NOC):
            base = oc * 131
            ve.tensor_scalar(xbc[:, oc * 128:(oc + 1) * 128],
                             xcat[:, base + 3:base + 131], cw[:, oc * 4 + 3:oc * 4 + 4],
                             cb[:, oc:oc + 1], op0=ALU.mult, op1=ALU.add)
            for j in range(3):
                ve.scalar_tensor_tensor(xbc[:, oc * 128:(oc + 1) * 128],
                                        xcat[:, base + j:base + j + 128],
                                        cw[:, oc * 4 + j:oc * 4 + j + 1],
                                        xbc[:, oc * 128:(oc + 1) * 128],
                                        op0=ALU.mult, op1=ALU.add)
        v17 = b.inc(ve.tensor_copy(sv[:, 6:7], sv[:, 5:6]), s_v, 1)
        se.wait_ge(s_v, v17)
        v18 = b.inc(se.activation(xbc[:], xbc[:], AF.Silu), s_a, 1)
        ve.wait_ge(s_a, v18)
        for oc in range(NOC):
            base = oc * 131
            ve.tensor_copy(xcat[:, base:base + 3],
                           xcat[:, base + 128:base + 131])
        v19 = b.inc(ve.tensor_copy(sv[:, 6:7], sv[:, 5:6]), s_v, 1)
        for k in range(17):
            b.wait(te, s_v, b.cnt[s_v] if k >= 1 else v19)
            v = b.inc(te.transpose(psT[:], xbc[:, k * 128:(k + 1) * 128], C["eye_f"][:]),
                      s_p, 1)
            ve.wait_ge(s_p, v)
            dst = Xp[:, k * 128:(k + 1) * 128] if k < 16 else Btm[:]
            b.inc(ve.tensor_copy(dst, psT[:]), s_v, 1)
        b.wait(ve, s_v)
        vx = b.inc(ve.tensor_tensor(
            Xp[:].rearrange("p (h e) -> p h e", e=HP),
            Xp[:].rearrange("p (h e) -> p h e", e=HP),
            dt_c.rearrange("p (h o) -> p h o", o=1).broadcast_to([128, NH, HP]),
            op=ALU.mult), s_v, 1)
        ve.wait_ge(s_v, vx)
        if own:
            C_fm = xbc[:, 17 * 128:18 * 128]
            b.wait(te, s_v, b.cnt[s_v])
            b.wait(te, s_a)
            for q in range(4):
                v = b.inc(te.matmul(psY[:, q * 512:(q + 1) * 512], C_fm,
                                    R[:, q * 512:(q + 1) * 512], start=True, stop=True),
                          s_t, 1)
            ve.wait_ge(s_t, b.cnt[s_t])
            voff = b.inc(ve.tensor_tensor(
                Ysb[:].rearrange("p (h e) -> p h e", e=HP),
                psY[:].rearrange("p (h e) -> p h e", e=HP),
                od_c.rearrange("p (h o) -> p h o", o=1).broadcast_to([128, NH, HP]),
                op=ALU.mult), s_v, 1)
            B_fm = xbc[:, 16 * 128:17 * 128]
            te.wait_ge(s_v, voff)
            vgt = b.inc(te.matmul(psT[:], B_fm, C_fm, start=True, stop=True), s_t, 1)
            ve.wait_ge(s_t, vgt)
            vgm = b.inc(ve.tensor_tensor(Gt[:], psT[:], C["triu_f"][:], op=ALU.mult),
                        s_v, 1)
            for h in range(NH):
                b.wait(te, s_v, b.cnt[s_v])
                v = b.inc(te.transpose(psT[:],
                                       acs_c[:, h:h + 1].broadcast_to([128, 128]),
                                       C["eye_f"][:]), s_p, 1)
                ve.wait_ge(s_p, v)
                vss = b.inc(ve.tensor_scalar(Mt[:], psT[:], acs_c[:, h:h + 1], 0.0,
                                             op0=ALU.subtract, op1=ALU.min), s_v, 1)
                se.wait_ge(s_v, vss)
                vex = b.inc(se.activation(Mt[:], Mt[:], AF.Exp), s_a, 1)
                ve.wait_ge(s_a, vex)
                vmt = b.inc(ve.tensor_mul(Mt[:], Mt[:], Gt[:]), s_v, 1)
                te.wait_ge(s_v, vmt)
                if h > 0:
                    b.wait(te, s_v, b.cnt[s_v])
                v = b.inc(te.matmul(psX[:, 0:HP], Mt[:],
                                    Xp[:, h * HP:(h + 1) * HP],
                                    start=True, stop=True), s_t, 1)
                ve.wait_ge(s_t, v)
                b.inc(ve.scalar_tensor_tensor(Ysb[:, h * HP:(h + 1) * HP],
                                              psX[:, 0:HP], 1.0,
                                              Ysb[:, h * HP:(h + 1) * HP],
                                              op0=ALU.mult, op1=ALU.add), s_v, 1)
            vy = b.inc(ve.tensor_copy(sv[:, 6:7], sv[:, 5:6]), s_v, 1)
        vxd = b.inc(ve.tensor_tensor(
            Xp[:].rearrange("p (h e) -> p h e", e=HP),
            Xp[:].rearrange("p (h e) -> p h e", e=HP),
            ds_c.rearrange("p (h o) -> p h o", o=1).broadcast_to([128, NH, HP]),
            op=ALU.mult), s_v, 1)
        ve.wait_ge(s_v, vxd)
        b.wait(te, s_v, b.cnt[s_v])
        b.wait(te, s_a)
        for q in range(4):
            v = b.inc(te.matmul(psY[:, q * 512:(q + 1) * 512], Btm[:],
                                Xp[:, q * 512:(q + 1) * 512], start=True, stop=True),
                      s_t, 1)
        ve.wait_ge(s_t, b.cnt[s_t])
        vre = b.inc(ve.tensor_tensor(
            R[:].rearrange("p (h e) -> p h e", e=HP),
            R[:].rearrange("p (h e) -> p h e", e=HP),
            E_c.rearrange("p (h o) -> p h o", o=1).broadcast_to([128, NH, HP]),
            op=ALU.mult), s_v, 1)
        ve.wait_ge(s_v, vre)
        vrs = b.inc(ve.scalar_tensor_tensor(R[:], psY[:], 1.0, R[:],
                                            op0=ALU.mult, op1=ALU.add), s_v, 1)
        ve.wait_ge(s_v, vrs)
        if c == 15:
            vpv = b.inc(ve.tensor_scalar(R[:], R[:], pv[:, 0:1], None, op0=ALU.mult),
                        s_v, 1)
            ve.wait_ge(s_v, vpv)
        if own:
            b.wait(te, s_v, b.cnt[s_v])
            for q in range(4):
                if q >= 1:
                    b.wait(te, s_a, b.cnt[s_a])
                vs_ = None
                for k in range(8):
                    vs_ = te.matmul(psD2[:], uT[:, k * 128:(k + 1) * 128],
                                    Wz_sb[:, k * 2048 + q * 512:k * 2048 + (q + 1) * 512],
                                    start=(k == 0), stop=(k == 7))
                v = b.inc(vs_, s_t, 1)
                se.wait_ge(s_t, v)
                b.inc(se.activation(zs[:, q * 512:(q + 1) * 512], psD2[:], AF.Silu),
                      s_a, 1)
            b.wait(ve, s_a)
            inst = ln_tile2(b, C, msc[:, 0:DI], Ysb[:], scr, sv, width=DI)
            ve.tensor_mul(msc[:, 0:DI], msc[:, 0:DI], nw_bc[:])
            vyg = b.inc(ve.tensor_mul(ygb[:], msc[:, 0:DI], zs[:]), s_v, 1)
            for k in range(16):
                b.wait(te, s_v, b.cnt[s_v] if k >= 1 else vyg)
                v = b.inc(te.transpose(psTb[:], ygb[:, k * 128:(k + 1) * 128],
                                       C["eye_bf"][:]), s_p, 1)
                ve.wait_ge(s_p, v)
                b.inc(ve.tensor_copy(ygT[:, k * 128:(k + 1) * 128], psTb[:]), s_v, 1)
            b.wait(te, s_v)
            for half in range(2):
                b.wait(te, s_v, b.cnt[s_v])
                vs_ = None
                for k in range(16):
                    vs_ = te.matmul(psD2[:],
                                    ygT[:, k * 128:(k + 1) * 128],
                                    Wo_sb[:, k * 1024 + half * 512:
                                           k * 1024 + (half + 1) * 512],
                                    start=(k == 0), stop=(k == 15))
                v = b.inc(vs_, s_t, 1)
                ve.wait_ge(s_t, v)
                b.inc(ve.tensor_copy(msc[:, D + half * 512:D + (half + 1) * 512],
                                     psD2[:]), s_v, 1)
            b.wait(ve, s_st)
            ln_tile2(b, C, msc[:, 0:D], msc[:, D:2 * D], scr, sv, width=D)
            v_o = b.inc(ve.tensor_add(msc[:, 0:D], msc[:, 0:D], xt[:]), s_v, 1)
            sy.wait_ge(s_v, v_o)
            b.inc(sy.dma_start(out=d["resid3"][row:row + 128, :], in_=msc[:, 0:D]),
                  s_st, 16)
    b.wait(sy, s_st)


def build_launch_a(nt=NT, stages=3, debug=False):
    nc = bass.Bass("TRN2", target_bir_lowering=False)
    dd = {}
    dd["x_own"] = nc.dram_tensor("x_own", [NTOK, D], F32, kind="ExternalInput").ap()
    dd["x_other"] = nc.dram_tensor("x_other", [NTOK, D], F32, kind="ExternalInput").ap()
    for w in ("Wq", "Wk", "Wv", "Wo"):
        dd[w] = nc.dram_tensor(w, [D, D], F32, kind="ExternalInput").ap()
    for bn in ("bq", "bk", "bv", "bo"):
        dd[bn] = nc.dram_tensor(bn, [D], F32, kind="ExternalInput").ap()
    dd["resid1"] = nc.dram_tensor("resid1", [NTOK, D], F32, kind="ExternalOutput").ap()
    if debug:
        dd["dbg_xt"] = nc.dram_tensor("dbg_xt", [128, D], mybir.dt.bfloat16, kind="ExternalOutput").ap()
        dd["dbg_q"] = nc.dram_tensor("dbg_q", [128, D], F32, kind="ExternalOutput").ap()
        dd["dbg_vl"] = nc.dram_tensor("dbg_vl", [128, D], mybir.dt.bfloat16, kind="ExternalOutput").ap()
        dd["dbg_at"] = nc.dram_tensor("dbg_at", [128, D], F32, kind="ExternalOutput").ap()
        dd["dbg_kvbc"] = nc.dram_tensor("dbg_kvbc", [128, 2 * D], F32, kind="ExternalOutput").ap()
    with ExitStack() as ctx:
        b = Bld(nc, ctx)
        C = setup_common(b)
        C["s_ln_a"] = b.sem("s_ln_a")
        C["s_ln_b"] = b.sem("s_ln_b")
        C["s_ln_c"] = b.sem("s_ln_c")
        emit_phase_a(b, C, dd, nt=nt, stages=stages)
    return nc


def build_launch_f0(debug=False):
    nc = bass.Bass("TRN2", target_bir_lowering=False)
    dd = {}
    dd["resid1"] = nc.dram_tensor("resid1", [NTOK, D], F32, kind="ExternalInput").ap()
    dd["gate_w"] = nc.dram_tensor("gate_w", [2, D], F32, kind="ExternalInput").ap()
    dd["w1"] = nc.dram_tensor("w1", [2, D, 4096], F32, kind="ExternalInput").ap()
    dd["b1"] = nc.dram_tensor("b1", [2, 4096], F32, kind="ExternalInput").ap()
    dd["w2"] = nc.dram_tensor("w2", [2, 4096, D], F32, kind="ExternalInput").ap()
    dd["b2"] = nc.dram_tensor("b2", [2, D], F32, kind="ExternalInput").ap()
    dd["sw1"] = nc.dram_tensor("sw1", [D, 4096], F32, kind="ExternalInput").ap()
    dd["sb1"] = nc.dram_tensor("sb1", [4096], F32, kind="ExternalInput").ap()
    dd["sw2"] = nc.dram_tensor("sw2", [4096, D], F32, kind="ExternalInput").ap()
    dd["sb2"] = nc.dram_tensor("sb2", [D], F32, kind="ExternalInput").ap()
    dd["resid2"] = nc.dram_tensor("resid2", [NTOK, D], F32, kind="ExternalOutput").ap()
    if debug:
        dd["masks_in"] = nc.dram_tensor("masks_in", [128, 2 * NT], F32, kind="ExternalInput").ap()
        dd["dbg_masks"] = nc.dram_tensor("dbg_masks", [128, 2 * NT], F32, kind="ExternalOutput").ap()
        dd["dbg_xeT"] = nc.dram_tensor("dbg_xeT", [128, NT * D], mybir.dt.bfloat16, kind="ExternalOutput").ap()
        dd["dbg_acc"] = nc.dram_tensor("dbg_acc", [128, NT * D], F32, kind="ExternalOutput").ap()
    with ExitStack() as ctx:
        b = Bld(nc, ctx)
        C = setup_common(b)
        C["s_ln_a"] = b.sem("s_ln_a")
        C["s_ln_b"] = b.sem("s_ln_b")
        C["s_ln_c"] = b.sem("s_ln_c")
        emit_phase_f0(b, C, dd)
    return nc


def build_launch_f1(npasses=9, sub=3):
    nc = bass.Bass("TRN2", target_bir_lowering=False)
    dd = {}
    dd["resid3"] = nc.dram_tensor("resid3", [NTOK, D], F32, kind="ExternalInput").ap()
    dd["ln_w"] = nc.dram_tensor("ln_w", [D], F32, kind="ExternalInput").ap()
    dd["ln_b"] = nc.dram_tensor("ln_b", [D], F32, kind="ExternalInput").ap()
    dd["router_w"] = nc.dram_tensor("router_w", [D, 8], F32, kind="ExternalInput").ap()
    dd["gate_w"] = nc.dram_tensor("gate_w", [8, D, 2048], F32, kind="ExternalInput").ap()
    dd["up_w"] = nc.dram_tensor("up_w", [8, D, 2048], F32, kind="ExternalInput").ap()
    dd["down_w"] = nc.dram_tensor("down_w", [8, 2048, D], F32, kind="ExternalInput").ap()
    dd["sh_gate"] = nc.dram_tensor("sh_gate", [D, 2048], F32, kind="ExternalInput").ap()
    dd["sh_up"] = nc.dram_tensor("sh_up", [D, 2048], F32, kind="ExternalInput").ap()
    dd["sh_down"] = nc.dram_tensor("sh_down", [2048, D], F32, kind="ExternalInput").ap()
    dd["out"] = nc.dram_tensor("out", [NTOK, D], F32, kind="ExternalOutput").ap()
    with ExitStack() as ctx:
        b = Bld(nc, ctx)
        C = setup_common(b)
        C["s_ln_a"] = b.sem("s_ln_a")
        C["s_ln_b"] = b.sem("s_ln_b")
        C["s_ln_c"] = b.sem("s_ln_c")
        emit_phase_f1(b, C, dd, npasses=npasses, sub=sub)
    return nc


def build_launch_m():
    nc = bass.Bass("TRN2", target_bir_lowering=False)
    dd = {}
    for n, s in [("resid2_prev", [NTOK, D]), ("resid2_own", [NTOK, D]),
                 ("prev_valid", [128, 1]), ("W_in", [D, 4384]), ("conv_w", [2304, 4]),
                 ("conv_b", [2304]), ("dt_bias", [32]), ("A_log", [32]),
                 ("norm_w", [2048]), ("W_out", [2048, D])]:
        dd[n] = nc.dram_tensor(n, s, F32, kind="ExternalInput").ap()
    dd["resid3"] = nc.dram_tensor("resid3", [NTOK, D], F32, kind="ExternalOutput").ap()
    with ExitStack() as ctx:
        b = Bld(nc, ctx)
        C = setup_common(b)
        C["s_ln_a"] = b.sem("s_ln_a")
        C["s_ln_b"] = b.sem("s_ln_b")
        C["s_ln_c"] = b.sem("s_ln_c")
        emit_phase_m(b, C, dd)
    return nc


_NC_CACHE = {}


def kernel(**inputs):
    z = {k: np.asarray(v) for k, v in inputs.items()}
    x = z["L0_attn_x"]
    if "A" not in _NC_CACHE:
        _NC_CACHE["A"] = build_launch_a()
        _NC_CACHE["F0"] = build_launch_f0()
        _NC_CACHE["M"] = build_launch_m()
        _NC_CACHE["F1"] = build_launch_f1()
    cores = list(range(8))

    def sl(a, c):
        bidx, half = c // 2, c % 2
        return np.ascontiguousarray(a[bidx, half * 2048:(half + 1) * 2048])

    im = []
    for c in cores:
        bidx, half = c // 2, c % 2
        im.append(dict(x_own=sl(x, c),
                       x_other=np.ascontiguousarray(
                           x[bidx, (1 - half) * 2048:(2 - half) * 2048]),
                       Wq=z["L0_attn_Wq"], Wk=z["L0_attn_Wk"], Wv=z["L0_attn_Wv"],
                       Wo=z["L0_attn_Wo"], bq=z["L0_attn_bq"], bk=z["L0_attn_bk"],
                       bv=z["L0_attn_bv"], bo=z["L0_attn_bo"]))
    r = run_bass_kernel_spmd(_NC_CACHE["A"], im, core_ids=cores)
    resid1 = [r.results[c]["resid1"] for c in cores]
    im = [dict(resid1=resid1[c], gate_w=z["L0_ffn_gate_w"], w1=z["L0_ffn_w1"],
               b1=z["L0_ffn_b1"], w2=z["L0_ffn_w2"], b2=z["L0_ffn_b2"],
               sw1=z["L0_ffn_sw1"], sb1=z["L0_ffn_sb1"], sw2=z["L0_ffn_sw2"],
               sb2=z["L0_ffn_sb2"]) for c in cores]
    r = run_bass_kernel_spmd(_NC_CACHE["F0"], im, core_ids=cores)
    resid2 = [r.results[c]["resid2"] for c in cores]
    zero = np.zeros((2048, 1024), np.float32)
    im = []
    for c in cores:
        half = c % 2
        prev = resid2[c - 1] if half == 1 else zero
        pv = np.full((128, 1), 1.0 if half == 1 else 0.0, np.float32)
        im.append(dict(resid2_prev=prev, resid2_own=resid2[c], prev_valid=pv,
                       W_in=z["L1_attn_W_in"], conv_w=z["L1_attn_conv_w"],
                       conv_b=z["L1_attn_conv_b"], dt_bias=z["L1_attn_dt_bias"],
                       A_log=z["L1_attn_A_log"], norm_w=z["L1_attn_norm_w"],
                       W_out=z["L1_attn_W_out"]))
    r = run_bass_kernel_spmd(_NC_CACHE["M"], im, core_ids=cores)
    resid3 = [r.results[c]["resid3"] for c in cores]
    im = [dict(resid3=resid3[c], ln_w=z["L1_ffn_ln_w"], ln_b=z["L1_ffn_ln_b"],
               router_w=z["L1_ffn_router_w"], gate_w=z["L1_ffn_gate_w"],
               up_w=z["L1_ffn_up_w"], down_w=z["L1_ffn_down_w"],
               sh_gate=z["L1_ffn_sh_gate"], sh_up=z["L1_ffn_sh_up"],
               sh_down=z["L1_ffn_sh_down"]) for c in cores]
    r = run_bass_kernel_spmd(_NC_CACHE["F1"], im, core_ids=cores)
    out = np.zeros((4, 4096, 1024), np.float32)
    for c in cores:
        bidx, half = c // 2, c % 2
        out[bidx, half * 2048:(half + 1) * 2048] = r.results[c]["out"]
    return out
